# version 21
# baseline (speedup 1.0000x reference)
import os
import numpy as np
import concourse.bass as bass
import concourse.mybir as mybir
from concourse.bass_utils import run_bass_kernel_spmd

F32 = mybir.dt.float32
BF16 = mybir.dt.bfloat16
F16 = mybir.dt.float16
ALU = mybir.AluOpType
ACT = mybir.ActivationFunctionType
AXX = mybir.AxisListType.X

B, S, D, I = 32, 2048, 1024, 2816
NCORES = 8
BPC = B // NCORES
T = 256
NG = T // 128
KPS = S // T
NT = BPC * KPS
DCH = D // 128
ICH = I // 128
IH = I // 2
NBH = IH // 16
SL512 = [(0, 512), (512, 512), (1024, 512), (1536, 512), (2048, 512), (2560, 256)]
NSL = len(SL512)

ENGS = ("sync", "vector", "scalar", "tensor", "gpsimd")

SIM_DRAINS = os.environ.get("BASS_SIM_FP64") == "1"
KDUMP = os.environ.get("KDUMP", "")


class Sch:

    def __init__(self):
        self.ops = {e: [] for e in ENGS}
        self.cnt = {e: 0 for e in ENGS}
        self.dma = {"s0": 0, "s1": 0, "A": 0, "o0": 0, "o1": 0}

    def on(self, eng, fn):
        self.ops[eng].append(("op", fn))

    def wait(self, eng, prod_eng, val):
        if eng == prod_eng:
            return
        if val and val > 0:
            self.ops[eng].append(("wait", prod_eng, val))

    def waitd(self, eng, which, val):
        if val and val > 0:
            self.ops[eng].append(("waitd", which, val))

    def ck(self, eng):
        self.cnt[eng] += 1
        self.ops[eng].append(("ck", self.cnt[eng]))
        if eng in ("vector", "scalar", "gpsimd"):
            self.ops[eng].append(("sdrain",))
        return self.cnt[eng]

    def sdrain(self, eng):
        self.ops[eng].append(("sdrain",))

    def rdrain(self, eng):
        self.ops[eng].append(("rdrain",))

    def dma_on(self, eng, which, fn):
        self.dma[which] += 16
        self.ops[eng].append(("dma", which, fn))
        return self.dma[which]


def build_program():
    nc = bass.Bass("TRN2", target_bir_lowering=False)
    x_h = nc.dram_tensor("x", [BPC * S, D], F32, kind="ExternalInput").ap()
    a_h = nc.dram_tensor("A", [D], F32, kind="ExternalInput").ap()
    bc_h = nc.dram_tensor("BC", [D, D], F32, kind="ExternalInput").ap()
    wg_h = nc.dram_tensor("WG", [I, D], F32, kind="ExternalInput").ap()
    wu_h = nc.dram_tensor("WU", [I, D], F32, kind="ExternalInput").ap()
    wd_h = nc.dram_tensor("WD", [D, I], F32, kind="ExternalInput").ap()
    y_h = nc.dram_tensor("y", [BPC * S, D], F32, kind="ExternalOutput").ap()

    sch = Sch()
    from contextlib import ExitStack
    with ExitStack() as es:
        sb = lambda n, sh, dt: es.enter_context(nc.sbuf_tensor(n, sh, dt))
        ps = lambda n, sh: es.enter_context(nc.psum_tensor(n, sh, F32))

        wgT = sb("wgT", [128, DCH * I], BF16)
        wuT = sb("wuT", [128, DCH * I], BF16)
        wdT = sb("wdT", [128, ICH * D], BF16)
        bcq = sb("bcq", [128, DCH * D], BF16)
        a_sb = sb("a_sb", [128, DCH], F32)
        idn = sb("idn", [128, 128], BF16)
        eps_sb = sb("eps_sb", [128, 1], F32)

        xst = [sb(f"xst{i}", [128, IH], F32) for i in range(2)]
        x1f = [sb(f"x1f{i}", [128, D], F32) for i in range(2)]
        xn2 = sb("xn2", [128, D], F16)
        hs = sb("hs", [128, DCH * T], BF16)
        carry = sb("carry", [128, DCH], F32)
        xqf = sb("xqf", [128, DCH * T], BF16)
        hbuf2 = [sb(f"hbuf{i}", [128, I], BF16) for i in range(2)]
        hqb = sb("hqb", [128, IH], BF16)
        xq_tm = hqb
        hqf = sb("hqf", [128, 512], BF16)
        sil = sb("sil", [128, 512], BF16)
        tmpY = sb("tmpY", [128, IH], F32)
        tmpS = sb("tmpS", [128, IH], BF16)
        tmpB = sb("tmpB", [128, IH], BF16)
        xn = sb("xn", [128, D], BF16)
        bmax = sb("bmax", [128, NBH], F32)
        sblk = bmax
        rblk = sb("rblk", [128, NBH], F32)
        bn6 = sb("bn6", [128, 12], F32)
        mv = sb("mv", [128, 2], F32)
        inv = sb("inv", [128, 1], F32)
        negm = sb("negm", [128, 1], F32)
        nmi = sb("nmi", [128, 1], F32)

        pool = [es.enter_context(nc.psum_tensor("pool0", [128, 1024], BF16)),
                es.enter_context(nc.psum_tensor("pool1", [128, 1024], BF16))]
        pattn = [ps("pattn0", [128, 512]), ps("pattn1", [128, 512])]
        pgu = [ps("pgu0", [128, 512]), ps("pgu1", [128, 512])]
        pdown = [ps("pdn0", [128, 512]), ps("pdn1", [128, 512])]

        sems = {}
        for nsem in ("sV", "sS", "sP", "sG", "sSY", "dS0", "dS1", "dA", "dO0", "dO1"):
            sems[nsem] = es.enter_context(nc.semaphore(nsem))
        ESEM = {"vector": "sV", "scalar": "sS", "tensor": "sP", "gpsimd": "sG",
                "sync": "sSY"}
        DSEM = {"s0": "dS0", "s1": "dS1", "A": "dA", "o0": "dO0", "o1": "dO1"}

        st_ = {
            "pool": [None, None],
            "pool_extra": None,
            "last_q_g": 0,
            "last_q_v": 0,
            "last_ln_s": 0,
        }

        pool_users = {0: {}, 1: {}}

        def pool_wait(banks):
            for b in banks:
                for e, v in pool_users[b].items():
                    sch.wait("tensor", e, v)
                pool_users[b] = {}
            if st_["pool_extra"] is not None:
                e, v = st_["pool_extra"]
                sch.wait("tensor", e, v)
                st_["pool_extra"] = None

        def pool_consumed(bank, eng, ck):
            pool_users[bank][eng] = max(pool_users[bank].get(eng, 0), ck)

        def slot(c, goff=0):
            return pool[c // 4][:, (c % 4) * 256 + goff:(c % 4) * 256 + goff + 128]

        def quant(src_t, src_off, W, dst_t, dst_off, src_ck=None, dst_free_ck=None,
                  hook_after_y=None):
            NB = W // 16
            s3 = lambda t, off: t[:, off:off + W].rearrange("p (b k) -> p b k", k=16)
            if src_ck is not None:
                sch.wait("vector", src_ck[0], src_ck[1])
            sch.rdrain("vector")
            sch.on("vector", lambda v, a=s3(src_t, src_off):
                   v.tensor_reduce(out=bmax[:, :NB], in_=a, axis=AXX, op=ALU.max,
                                   apply_absolute_value=True))
            sch.rdrain("vector")
            sch.on("vector", lambda v: v.tensor_scalar(
                out=sblk[:, :NB], in0=bmax[:, :NB], scalar1=1.0 / 6.0, scalar2=1e-12,
                op0=ALU.mult, op1=ALU.max))
            sch.rdrain("vector")
            sch.on("vector", lambda v: v.reciprocal(out=rblk[:, :NB], in_=sblk[:, :NB]))
            sch.rdrain("vector")
            rb3 = rblk[:, :NB].unsqueeze(2).broadcast_to([128, NB, 16])
            sch.on("vector", lambda v, a=s3(src_t, src_off), r=rb3, o=s3(tmpY, 0):
                   v.tensor_tensor(out=o, in0=a, in1=r, op=ALU.mult))
            y_ck = sch.ck("vector")
            if hook_after_y is not None:
                hook_after_y()

            sch.wait("scalar", "vector", y_ck)
            sch.on("scalar", lambda s: s.activation(out=tmpS[:, :W], in_=tmpY[:, :W],
                                                    func=ACT.Sign))
            sch.sdrain("scalar")
            sch.on("scalar", lambda s: s.activation(out=tmpY[:, :W], in_=tmpY[:, :W],
                                                    func=ACT.Abs))
            cks = sch.ck("scalar")

            sch.wait("vector", "scalar", cks)
            if dst_free_ck is not None:
                sch.wait("vector", dst_free_ck[0], dst_free_ck[1])
            B1 = tmpB[:, :W]
            B2 = dst_t[:, dst_off:dst_off + W]
            YY = tmpY[:, :W]
            sch.on("vector", lambda v: v.tensor_scalar(out=B2, in0=YY, scalar1=0.25,
                                                       scalar2=0.5, op0=ALU.is_gt,
                                                       op1=ALU.mult))
            for srcap, th, w in ((YY, 0.75, 0.5), (YY, 1.25, 0.5), (YY, 1.75, 0.5),
                                 (YY, 2.5, 1.0), (YY, 3.5, 1.0), (YY, 5.0, 2.0)):
                sch.on("vector", lambda v, srcap=srcap, th=th, w=w: v.tensor_scalar(
                    out=B1, in0=srcap, scalar1=th, scalar2=w, op0=ALU.is_gt,
                    op1=ALU.mult))
                sch.sdrain("vector")
                sch.on("vector", lambda v: v.tensor_tensor(out=B2, in0=B2, in1=B1,
                                                           op=ALU.add))
                sch.sdrain("vector")
            ckv2 = sch.ck("vector")

            sch.on("vector", lambda v: v.tensor_tensor(
                out=tmpB[:, :W], in0=dst_t[:, dst_off:dst_off + W],
                in1=tmpS[:, :W], op=ALU.mult))
            sch.sdrain("vector")
            sb3 = sblk[:, :NB].unsqueeze(2).broadcast_to([128, NB, 16])
            sch.on("vector", lambda v, sc=sb3, o=s3(dst_t, dst_off):
                   v.tensor_tensor(out=o, in0=s3(tmpB, 0), in1=sc, op=ALU.mult))
            q_ck = sch.ck("vector")
            st_["last_q_g"] = q_ck
            return y_ck, q_ck

        def ln_stats(src_t, src_off, W, prod_ck):
            if prod_ck is not None:
                sch.wait("vector", prod_ck[0], prod_ck[1])
            sch.wait("vector", "scalar", st_["last_ln_s"])
            sch.rdrain("vector")
            sch.on("vector", lambda v: v.bn_stats(out=bn6[:, 0:6],
                                                  in_=src_t[:, src_off:src_off + 512]))
            sch.on("vector", lambda v: v.bn_stats(out=bn6[:, 6:12],
                                                  in_=src_t[:, src_off + 512:src_off + W]))
            sch.rdrain("vector")
            sch.on("vector", lambda v: v.bn_aggr(out=mv[:], in_=bn6[:]))
            sch.rdrain("vector")
            sch.on("vector", lambda v: v.tensor_scalar(out=negm[:], in0=mv[:, 0:1],
                                                       scalar1=-1.0, scalar2=None,
                                                       op0=ALU.mult))
            sch.on("vector", lambda v: v.tensor_scalar(out=nmi[:], in0=mv[:, 1:2],
                                                       scalar1=1e-5, scalar2=None,
                                                       op0=ALU.add))
            sch.rdrain("vector")
            sch.on("vector", lambda v: v.reciprocal(out=mv[:, 1:2], in_=nmi[:]))
            sch.rdrain("vector")
            sch.on("vector", lambda v: v.engine_nop())
            ckv = sch.ck("vector")
            sch.wait("scalar", "vector", ckv)
            sch.on("scalar", lambda s: s.activation(out=inv[:], in_=mv[:, 1:2],
                                                    func=ACT.Sqrt))
            sck = sch.ck("scalar")
            st_["last_ln_s"] = sck
            sch.wait("vector", "scalar", sck)
            sch.on("vector", lambda v: v.tensor_scalar(
                out=nmi[:], in0=negm[:], scalar1=inv[:, 0:1], scalar2=None,
                op0=ALU.mult))
            sch.sdrain("vector")
            return sck

        with nc.sbuf_tensor("iot", [128, 128], BF16) as iot, \
             nc.sbuf_tensor("iop", [128, 128], BF16) as iop:
            wst = xst

            sch.on("gpsimd", lambda g: g.iota(iot[:], [[1, 128]], channel_multiplier=0,
                                              allow_small_or_imprecise_dtypes=True))
            sch.on("gpsimd", lambda g: g.iota(iop[:], [[0, 128]], channel_multiplier=1,
                                              allow_small_or_imprecise_dtypes=True))
            ckg0 = sch.ck("gpsimd")
            sch.wait("vector", "gpsimd", ckg0)
            sch.on("vector", lambda v: v.tensor_tensor(out=idn[:], in0=iot[:], in1=iop[:],
                                                       op=ALU.is_equal))
            sch.on("vector", lambda v: v.memset(eps_sb[:], 1e-5))
            ck_idn = sch.ck("vector")
            st_["pool_extra"] = ("vector", ck_idn)

            vA = sch.dma_on("sync", "A", lambda e, sem: e.dma_start(
                out=a_sb[:], in_=a_h.rearrange("(c p) -> p c", p=128),
                allow_slow_non_contiguous=True).then_inc(sem, 16))

            wst_free = [None, None]
            bc_cast = {}
            for c in range(DCH):
                if wst_free[c % 2] is not None:
                    sch.wait("sync", wst_free[c % 2][0], wst_free[c % 2][1])
                vL = sch.dma_on("sync", f"s{c % 2}", lambda e, sem, c=c: e.dma_start(
                    out=wst[c % 2][:, :1024],
                    in_=bc_h[c * 128:(c + 1) * 128, :]).then_inc(sem, 16))
                sch.waitd("scalar", f"s{c % 2}", vL)
                sch.on("scalar", lambda s, c=c: s.copy(out=bcq[:, c * D:(c + 1) * D],
                                                       in_=wst[c % 2][:, :1024]))
                bc_cast[c] = sch.ck("scalar")
                wst_free[c % 2] = ("scalar", bc_cast[c])
            ck_bc = bc_cast[DCH - 1]

            hqb_pe = {"ck": 0}

            def prep_mat(src_fn, nrc, rowlen, dst_t, colbase, colstep):
                ntc = rowlen // 128
                for rc in range(nrc):
                    stg = wst[rc % 2]
                    if wst_free[rc % 2] is not None:
                        sch.wait("sync", wst_free[rc % 2][0], wst_free[rc % 2][1])
                    vL = sch.dma_on("sync", f"s{rc % 2}", lambda e, sem, stg=stg, rc=rc:
                                    e.dma_start(out=stg[:, :rowlen],
                                                in_=src_fn(rc)).then_inc(sem, 16))
                    sch.waitd("vector", f"s{rc % 2}", vL)
                    y_ck, q_ck = quant(stg, 0, rowlen, hqb, 0,
                                       dst_free_ck=("tensor", hqb_pe["ck"]))
                    wst_free[rc % 2] = ("vector", y_ck)
                    for tc0 in range(0, ntc, 8):
                        nt_b = min(8, ntc - tc0)
                        banks = sorted({k // 4 for k in range(nt_b)})
                        pool_wait(banks)
                        if tc0 == 0:
                            sch.wait("tensor", "vector", q_ck)
                        for k in range(nt_b):
                            sch.on("tensor", lambda t, tc=tc0 + k, k=k: t.transpose(
                                slot(k), hqb[:, tc * 128:(tc + 1) * 128], idn[:]))
                        ckp = sch.ck("tensor")
                        hqb_pe["ck"] = ckp
                        sch.wait("scalar", "tensor", ckp)
                        for b in banks:
                            nb = min(4, nt_b - b * 4)
                            inap = pool[b][:].rearrange("p (n x) -> p n x",
                                                        x=256)[:, 0:nb, 0:128]
                            n0 = colbase // colstep + tc0 + b * 4
                            outap = dst_t[:].rearrange("p (n x) -> p n x",
                                                       x=colstep)[:, n0:n0 + nb,
                                                                  rc * 128:rc * 128 + 128]
                            sch.on("scalar", lambda s, i=inap, o=outap: s.copy(
                                out=o, in_=i))
                        cke = sch.ck("scalar")
                        for b in banks:
                            pool_consumed(b, "scalar", cke)

            prep_mat(lambda rc: wg_h[rc * 128:(rc + 1) * 128, :], ICH, 1024,
                     wgT, 0, I)
            prep_mat(lambda rc: wu_h[rc * 128:(rc + 1) * 128, :], ICH, 1024,
                     wuT, 0, I)
            for half in range(2):
                prep_mat(lambda rc, h=half: wd_h[rc * 128:(rc + 1) * 128,
                                                 h * IH:(h + 1) * IH], DCH, IH,
                         wdT, half * 11 * D, D)

        prep_done = {e: sch.cnt[e] for e in ENGS}

        stg_readers = {}
        hs_attn_ck = {}
        attn_free = {"ck": 0}
        gu_mult_ck = {}
        xn2_y = {"ck": 0}
        hbuf_y = {"ck": 0}
        xn_pe = {"ck": 0}
        pdown_free = {"ck": 0}
        out_dma = [0, 0]
        dma_j = {}
        xqf_ck = {}

        def emit_xqT(t, g, q_ck):
            pool_wait([0, 1])
            sch.wait("tensor", "vector", q_ck)
            if t == 0 and g == 0:
                sch.wait("tensor", "scalar", prep_done["scalar"])
            for c in range(DCH):
                sch.on("tensor", lambda tt, c=c, g=g: tt.transpose(
                    slot(c, g * 128), xq_tm[:, c * 128:(c + 1) * 128], idn[:]))
            ck_xqT = sch.ck("tensor")
            hqb_pe["ck"] = ck_xqT
            sch.wait("scalar", "tensor", ck_xqT)
            for b in range(2):
                inap = pool[b][:].rearrange("p (n x) -> p n x",
                                            x=256)[:, 0:4, g * 128:g * 128 + 128]
                outap = xqf[:].rearrange("p (n x) -> p n x",
                                         x=T)[:, b * 4:b * 4 + 4,
                                              g * 128:g * 128 + 128]
                sch.on("scalar", lambda s, i=inap, o=outap: s.copy(out=o, in_=i))
            ck = sch.ck("scalar")
            xqf_ck[g] = ck
            pool_consumed(0, "scalar", ck)
            pool_consumed(1, "scalar", ck)

        for t in range(NT):
            seq, kk = divmod(t, KPS)
            row0 = seq * S + kk * T

            for g in range(NG):
                j = NG * t + g
                stg = xst[j % 2]
                if j >= 2:
                    for (e, v) in stg_readers[j - 2]:
                        sch.wait("sync", e, v)
                elif wst_free[j % 2] is not None:
                    sch.wait("sync", wst_free[j % 2][0], wst_free[j % 2][1])
                dma_j[j] = sch.dma_on("sync", f"s{j % 2}", lambda e, sem, stg=stg,
                                      r0=row0 + g * 128: e.dma_start(
                    out=stg[:, :D], in_=x_h[r0:r0 + 128, :]).then_inc(sem, 16))

            tck = {}
            for g in range(NG):
                j = NG * t + g
                sch.waitd("vector", f"s{j % 2}", dma_j[j])
                if t == 0 and g == 0:
                    sch.wait("vector", "gpsimd", prep_done["gpsimd"])
                ln_stats(xst[j % 2], 0, D, None)
                sch.wait("vector", "tensor", xn_pe["ck"])
                sch.on("vector", lambda v, stg=xst[j % 2]: v.scalar_tensor_tensor(
                    out=xn[:], in0=stg[:, :D], scalar=inv[:, 0:1],
                    in1=nmi[:].broadcast_to([128, D]), op0=ALU.mult, op1=ALU.add))
                ck_xn = sch.ck("vector")
                stg_readers[j] = [("vector", ck_xn)]
                if g == 0:
                    pool_wait([0, 1])
                sch.wait("tensor", "vector", ck_xn)
                for c in range(DCH):
                    sch.on("tensor", lambda tt, c=c, g=g: tt.transpose(
                        slot(c, g * 128), xn[:, c * 128:(c + 1) * 128], idn[:]))
                tck[g] = sch.ck("tensor")
                xn_pe["ck"] = tck[g]

            sch.wait("vector", "tensor", tck[1])
            if t >= 1:
                sch.wait("vector", "tensor", hs_attn_ck[t - 1])
            if t == 0:
                sch.waitd("vector", "A", vA)
            sch.rdrain("vector")
            for c in range(DCH):
                init = 0.0 if kk == 0 else carry[:, c:c + 1]
                a0 = a_sb[:, c:c + 1].broadcast_to([128, 256])
                psrc = pool[c // 4][:].rearrange("p (n x) -> p n x",
                                                 x=256)[:, c % 4, :]
                sch.on("vector", lambda v, c=c, init=init, a0=a0, psrc=psrc:
                       v.tensor_tensor_scan(out=hs[:, c * T:c * T + 256],
                                            data0=a0, data1=psrc, initial=init,
                                            op0=ALU.mult, op1=ALU.add))
            sch.rdrain("vector")
            sch.on("vector", lambda v: v.tensor_copy(
                out=carry[:].unsqueeze(2),
                in_=hs[:].rearrange("p (c t) -> p c t", t=T)[:, :, T - 1:T]))
            ck_scan = sch.ck("vector")
            pool_consumed(0, "vector", ck_scan)
            pool_consumed(1, "vector", ck_scan)

            radd_ck = {}
            for g in range(NG):
                j = NG * t + g
                sch.wait("tensor", "vector", ck_scan)
                if t == 0 and g == 0:
                    sch.wait("tensor", "scalar", ck_bc)
                sch.wait("tensor", "vector", attn_free["ck"])
                for os in range(2):
                    for c in range(DCH):
                        sch.on("tensor", lambda tt, os=os, c=c, g=g: tt.matmul(
                            pattn[os][:], hs[:, c * T + g * 128:c * T + (g + 1) * 128],
                            bcq[:, c * D + os * 512:c * D + os * 512 + 512],
                            start=(c == 0), stop=(c == DCH - 1)))
                ck_attn = sch.ck("tensor")
                if g == NG - 1:
                    hs_attn_ck[t] = ck_attn

                sch.wait("vector", "tensor", ck_attn)
                sch.waitd("vector", f"o{g}", out_dma[g])
                for os in range(2):
                    sch.on("vector", lambda v, os=os, g=g, stg=xst[j % 2]:
                           v.tensor_tensor(out=x1f[g][:, os * 512:os * 512 + 512],
                                           in0=stg[:, os * 512:os * 512 + 512],
                                           in1=pattn[os][:], op=ALU.add))
                radd_ck[g] = sch.ck("vector")
                attn_free["ck"] = radd_ck[g]
                stg_readers[j].append(("vector", radd_ck[g]))

            qck = {}
            xn2ck = {}

            def emit_ln2(g):
                ln_stats(x1f[g], 0, D, ("vector", radd_ck[g]))
                sch.on("vector", lambda v, g=g: v.scalar_tensor_tensor(
                    out=xn2[:], in0=x1f[g][:], scalar=inv[:, 0:1],
                    in1=nmi[:].broadcast_to([128, D]), op0=ALU.mult, op1=ALU.add))
                xn2ck[g] = sch.ck("vector")

            emit_ln2(0)
            y_ck, qck[0] = quant(xn2, 0, D, xq_tm, 0,
                                 src_ck=("vector", xn2ck[0]),
                                 dst_free_ck=("tensor", hqb_pe["ck"]),
                                 hook_after_y=lambda: emit_ln2(1))
            emit_xqT(t, 0, qck[0])
            for g in range(1, NG):
                y_ck, qck[g] = quant(xn2, 0, D, xq_tm, 0,
                                     src_ck=("vector", xn2ck[g]),
                                     dst_free_ck=("tensor", hqb_pe["ck"]))

            def gu_slices(g, mid_hook=None):
                for si, (soff, sw) in enumerate(SL512):
                    pg = (pgu, pattn, pdown)[si % 3]
                    sch.wait("tensor", "scalar", xqf_ck[g])
                    prevmult = gu_mult_ck.get((g, si - 3)) if si >= 3 else (
                        gu_mult_ck.get((1 - g, NSL - 3 + si)))
                    if prevmult:
                        sch.wait("tensor", "vector", prevmult)
                    if si in (0, 1):
                        sch.wait("tensor", "vector", attn_free["ck"])
                    if si == 2:
                        sch.wait("tensor", "vector", pdown_free["ck"])
                    for c in range(DCH):
                        sch.on("tensor", lambda tt, c=c, g=g, soff=soff, sw=sw,
                               pg=pg: tt.matmul(
                            pg[0][:, :sw], xqf[:, c * T + g * 128:c * T + (g + 1) * 128],
                            wgT[:, c * I + soff:c * I + soff + sw],
                            start=(c == 0), stop=(c == DCH - 1)))
                    for c in range(DCH):
                        sch.on("tensor", lambda tt, c=c, g=g, soff=soff, sw=sw,
                               pg=pg: tt.matmul(
                            pg[1][:, :sw], xqf[:, c * T + g * 128:c * T + (g + 1) * 128],
                            wuT[:, c * I + soff:c * I + soff + sw],
                            start=(c == 0), stop=(c == DCH - 1)))
                    ck_gu = sch.ck("tensor")
                    sch.wait("scalar", "tensor", ck_gu)
                    prevmult1 = gu_mult_ck.get((g, si - 1)) if si >= 1 else (
                        gu_mult_ck.get((1 - g, NSL - 1)))
                    if prevmult1:
                        sch.wait("scalar", "vector", prevmult1)
                    sch.on("scalar", lambda s, sw=sw, pg=pg: s.activation(
                        out=sil[:, :sw], in_=pg[0][:, :sw], func=ACT.Silu))
                    ck_sil = sch.ck("scalar")
                    sch.wait("vector", "scalar", ck_sil)
                    sch.on("vector", lambda v, g=g, soff=soff, sw=sw, pg=pg:
                           v.tensor_tensor(
                        out=hbuf2[g][:, soff:soff + sw], in0=sil[:, :sw],
                        in1=pg[1][:, :sw], op=ALU.mult))
                    gu_mult_ck[(g, si)] = sch.ck("vector")
                    if si == 3 and mid_hook is not None:
                        mid_hook()
                attn_free["ck"] = max(attn_free["ck"], gu_mult_ck[(g, NSL - 1)])
                pdown_free["ck"] = max(pdown_free["ck"], gu_mult_ck[(g, NSL - 1)])

            gu_slices(0)
            emit_xqT(t, 1, qck[1])
            hq_pre = {}

            def _pre_hq0():
                y_ck, q_ck = quant(hbuf2[0], 0, IH, hqb, 0,
                                   src_ck=("vector", gu_mult_ck[(0, NSL - 1)]),
                                   dst_free_ck=("tensor", hqb_pe["ck"]))
                hq_pre["q"] = q_ck

            gu_slices(1, mid_hook=_pre_hq0)

            def emit_out(g, dck):
                sch.wait("vector", "tensor", dck)
                for dsl in range(2):
                    sch.on("vector", lambda v, g=g, dsl=dsl: v.tensor_tensor(
                        out=x1f[g][:, dsl * 512:dsl * 512 + 512],
                        in0=x1f[g][:, dsl * 512:dsl * 512 + 512],
                        in1=pdown[dsl][:], op=ALU.add))
                eck = sch.ck("vector")
                pdown_free["ck"] = max(pdown_free["ck"], eck)
                sch.wait("sync", "vector", eck)
                r0g = row0 + g * 128
                out_dma[g] = sch.dma_on("sync", f"o{g}", lambda e, sem,
                                        r0g=r0g, g=g: e.dma_start(
                    out=y_h[r0g:r0g + 128, :], in_=x1f[g][:, :D]).then_inc(sem, 16))

            down_cks = {}
            for g in range(NG):
                j = NG * t + g
                down_ck = 0
                for half in range(2):
                    if g == 0 and half == 0:
                        hq_ck = hq_pre["q"]
                    else:
                        hook = None
                        if g == 1 and half == 0:
                            hook = lambda: emit_out(0, down_cks[0])
                        y_ck, hq_ck = quant(hbuf2[g], half * IH, IH, hqb, 0,
                                            src_ck=("vector",
                                                    gu_mult_ck[(g, NSL - 1)]),
                                            dst_free_ck=("tensor", hqb_pe["ck"]),
                                            hook_after_y=hook)
                    for bi, nb in enumerate((4, 4, 3)):
                        bank = bi % 2
                        pool_wait([bank])
                        if bi == 0:
                            sch.wait("tensor", "vector", hq_ck)
                        for k in range(nb):
                            sch.on("tensor", lambda tt, bi=bi, k=k, bank=bank:
                                   tt.transpose(pool[bank][:, k * 256:k * 256 + 128],
                                                hqb[:, (bi * 4 + k) * 128:
                                                    (bi * 4 + k) * 128 + 128], idn[:]))
                        ck_hqT = sch.ck("tensor")
                        hqb_pe["ck"] = ck_hqT
                        sch.wait("scalar", "tensor", ck_hqT)
                        if down_ck:
                            sch.wait("scalar", "tensor", down_ck)
                        inap = pool[bank][:].rearrange("p (n x) -> p n x",
                                                       x=256)[:, 0:nb, 0:128]
                        outap = hqf[:].rearrange("p (n x) -> p n x",
                                                 x=128)[:, 0:nb, :]
                        sch.on("scalar", lambda s, i=inap, o=outap: s.copy(out=o, in_=i))
                        ck_ev = sch.ck("scalar")
                        pool_consumed(bank, "scalar", ck_ev)
                        sch.wait("tensor", "scalar", ck_ev)
                        first = (half == 0 and bi == 0)
                        last = (half == 1 and bi == 2)
                        if first:
                            sch.wait("tensor", "vector", pdown_free["ck"])
                        for k in range(nb):
                            ic = half * 11 + bi * 4 + k
                            for dsl in range(2):
                                sch.on("tensor", lambda tt, k=k, ic=ic, dsl=dsl,
                                       st=(first and k == 0),
                                       sp=(last and k == nb - 1): tt.matmul(
                                    pdown[dsl][:], hqf[:, k * 128:k * 128 + 128],
                                    wdT[:, ic * D + dsl * 512:ic * D + dsl * 512 + 512],
                                    start=st, stop=sp))
                        down_ck = sch.ck("tensor")
                down_cks[g] = down_ck
            emit_out(1, down_cks[1])

        sch.on("sync", lambda e, t0=sch.dma["o0"]: e.wait_ge(sems["dO0"], t0))
        sch.on("sync", lambda e, t1=sch.dma["o1"]: e.wait_ge(sems["dO1"], t1))

        with nc.Block() as block:
            sim_drains = SIM_DRAINS

            def run(e, eng):
                last = [None]
                my = ESEM[eng]
                for item in sch.ops[eng]:
                    kind = item[0]
                    if kind == "op":
                        r = item[1](e)
                        if r is not None:
                            last[0] = r
                    elif kind == "wait":
                        e.wait_ge(sems[ESEM[item[1]]], item[2])
                    elif kind == "waitd":
                        e.wait_ge(sems[DSEM[item[1]]], item[2])
                    elif kind == "sdrain":
                        if sim_drains and eng != "sync":
                            e.drain()
                    elif kind == "rdrain":
                        e.drain()
                    elif kind == "ck":
                        if eng == "sync":
                            e.nop().then_inc(sems[my], 1)
                        elif last[0] is not None:
                            last[0].then_inc(sems[my], 1)
                            last[0] = None
                        else:
                            e.drain().then_inc(sems[my], 1)
                    elif kind == "dma":
                        item[2](e, sems[DSEM[item[1]]])

            @block.sync
            def _(e):
                run(e, "sync")

            @block.vector
            def _(e):
                run(e, "vector")

            @block.scalar
            def _(e):
                run(e, "scalar")

            @block.tensor
            def _(e):
                run(e, "tensor")

            @block.gpsimd
            def _(e):
                run(e, "gpsimd")

    return nc


_NC_CACHE = {}


def kernel(L0_attn_inputs, L0_attn_A, L0_attn_BC, L0_ffn_w_gate, L0_ffn_w_up,
           L0_ffn_w_down):
    x = np.ascontiguousarray(np.asarray(L0_attn_inputs, dtype=np.float32))
    A = np.ascontiguousarray(np.asarray(L0_attn_A, dtype=np.float32))
    BC = np.ascontiguousarray(np.asarray(L0_attn_BC, dtype=np.float32))
    WG = np.ascontiguousarray(np.asarray(L0_ffn_w_gate, dtype=np.float32))
    WU = np.ascontiguousarray(np.asarray(L0_ffn_w_up, dtype=np.float32))
    WD = np.ascontiguousarray(np.asarray(L0_ffn_w_down, dtype=np.float32))

    if "nc" not in _NC_CACHE:
        _NC_CACHE["nc"] = build_program()
    nc = _NC_CACHE["nc"]

    in_maps = []
    for c in range(NCORES):
        xs = np.ascontiguousarray(x[c * BPC:(c + 1) * BPC].reshape(BPC * S, D))
        in_maps.append({"x": xs, "A": A, "BC": BC, "WG": WG, "WU": WU, "WD": WD})

    res = run_bass_kernel_spmd(nc, in_maps, core_ids=list(range(NCORES)))
    out = np.concatenate([r["y"].reshape(BPC, S, D) for r in res.results], axis=0)
    return out


# version 23
# speedup vs baseline: 1.0203x; 1.0203x over previous
import os
import numpy as np
import concourse.bass as bass
import concourse.mybir as mybir
from concourse.bass_utils import run_bass_kernel_spmd

F32 = mybir.dt.float32
BF16 = mybir.dt.bfloat16
F16 = mybir.dt.float16
ALU = mybir.AluOpType
ACT = mybir.ActivationFunctionType
AXX = mybir.AxisListType.X

B, S, D, I = 32, 2048, 1024, 2816
NCORES = 8
BPC = B // NCORES
T = 256
NG = T // 128
KPS = S // T
NT = BPC * KPS
DCH = D // 128
ICH = I // 128
IH = I // 2
NBH = IH // 16
SL512 = [(0, 512), (512, 512), (1024, 512), (1536, 512), (2048, 512), (2560, 256)]
NSL = len(SL512)

ENGS = ("sync", "vector", "scalar", "tensor", "gpsimd")

SIM_DRAINS = os.environ.get("BASS_SIM_FP64") == "1"
KDUMP = os.environ.get("KDUMP", "")


class Sch:

    def __init__(self):
        self.ops = {e: [] for e in ENGS}
        self.cnt = {e: 0 for e in ENGS}
        self.dma = {"s0": 0, "s1": 0, "A": 0, "o0": 0, "o1": 0}

    def on(self, eng, fn):
        self.ops[eng].append(("op", fn))

    def wait(self, eng, prod_eng, val):
        if eng == prod_eng:
            return
        if val and val > 0:
            self.ops[eng].append(("wait", prod_eng, val))

    def waitd(self, eng, which, val):
        if val and val > 0:
            self.ops[eng].append(("waitd", which, val))

    def ck(self, eng):
        self.cnt[eng] += 1
        self.ops[eng].append(("ck", self.cnt[eng]))
        if eng in ("vector", "scalar", "gpsimd"):
            self.ops[eng].append(("sdrain",))
        return self.cnt[eng]

    def sdrain(self, eng):
        self.ops[eng].append(("sdrain",))

    def rdrain(self, eng):
        self.ops[eng].append(("rdrain",))

    def dma_on(self, eng, which, fn):
        self.dma[which] += 16
        self.ops[eng].append(("dma", which, fn))
        return self.dma[which]


def build_program():
    nc = bass.Bass("TRN2", target_bir_lowering=False)
    x_h = nc.dram_tensor("x", [BPC * S, D], F32, kind="ExternalInput").ap()
    a_h = nc.dram_tensor("A", [D], F32, kind="ExternalInput").ap()
    bc_h = nc.dram_tensor("BC", [D, D], F32, kind="ExternalInput").ap()
    wg_h = nc.dram_tensor("WG", [I, D], F32, kind="ExternalInput").ap()
    wu_h = nc.dram_tensor("WU", [I, D], F32, kind="ExternalInput").ap()
    wd_h = nc.dram_tensor("WD", [D, I], F32, kind="ExternalInput").ap()
    y_h = nc.dram_tensor("y", [BPC * S, D], F32, kind="ExternalOutput").ap()

    sch = Sch()
    from contextlib import ExitStack
    with ExitStack() as es:
        sb = lambda n, sh, dt: es.enter_context(nc.sbuf_tensor(n, sh, dt))
        ps = lambda n, sh: es.enter_context(nc.psum_tensor(n, sh, F32))

        wgT = sb("wgT", [128, DCH * I], BF16)
        wuT = sb("wuT", [128, DCH * I], BF16)
        wdT = sb("wdT", [128, ICH * D], BF16)
        bcq = sb("bcq", [128, DCH * D], BF16)
        a_sb = sb("a_sb", [128, DCH], F32)
        idn = sb("idn", [128, 128], BF16)
        eps_sb = sb("eps_sb", [128, 1], F32)

        xst = [sb(f"xst{i}", [128, IH], F32) for i in range(2)]
        x1f = [sb(f"x1f{i}", [128, D], F32) for i in range(2)]
        xn2 = sb("xn2", [128, D], F16)
        hs = sb("hs", [128, DCH * T], BF16)
        carry = sb("carry", [128, DCH], F32)
        xqf = sb("xqf", [128, DCH * T], BF16)
        hbuf2 = [sb(f"hbuf{i}", [128, I], BF16) for i in range(2)]
        hqb = sb("hqb", [128, IH], BF16)
        xq_tm = hqb
        hqf = sb("hqf", [128, 512], BF16)
        sil = sb("sil", [128, 512], BF16)
        tmpY = sb("tmpY", [128, IH], F32)
        tmpS = sb("tmpS", [128, IH], BF16)
        tmpB = sb("tmpB", [128, IH], BF16)
        xn = sb("xn", [128, D], BF16)
        bmax = sb("bmax", [128, NBH], F32)
        sblk = bmax
        rblk = sb("rblk", [128, NBH], F32)
        bn6 = sb("bn6", [128, 12], F32)
        mv = sb("mv", [128, 2], F32)
        inv = sb("inv", [128, 1], F32)
        negm = sb("negm", [128, 1], F32)
        nmi = sb("nmi", [128, 1], F32)

        pool = [es.enter_context(nc.psum_tensor("pool0", [128, 1024], BF16)),
                es.enter_context(nc.psum_tensor("pool1", [128, 1024], BF16))]
        pattn = [ps("pattn0", [128, 512]), ps("pattn1", [128, 512])]
        pgu = [ps("pgu0", [128, 512]), ps("pgu1", [128, 512])]
        pdown = [ps("pdn0", [128, 512]), ps("pdn1", [128, 512])]

        sems = {}
        for nsem in ("sV", "sS", "sP", "sG", "sSY", "dS0", "dS1", "dA", "dO0", "dO1"):
            sems[nsem] = es.enter_context(nc.semaphore(nsem))
        ESEM = {"vector": "sV", "scalar": "sS", "tensor": "sP", "gpsimd": "sG",
                "sync": "sSY"}
        DSEM = {"s0": "dS0", "s1": "dS1", "A": "dA", "o0": "dO0", "o1": "dO1"}

        st_ = {
            "pool": [None, None],
            "pool_extra": None,
            "last_q_g": 0,
            "last_q_v": 0,
            "last_ln_s": 0,
        }

        pool_users = {0: {}, 1: {}}

        def pool_wait(banks):
            for b in banks:
                for e, v in pool_users[b].items():
                    sch.wait("tensor", e, v)
                pool_users[b] = {}
            if st_["pool_extra"] is not None:
                e, v = st_["pool_extra"]
                sch.wait("tensor", e, v)
                st_["pool_extra"] = None

        def pool_consumed(bank, eng, ck):
            pool_users[bank][eng] = max(pool_users[bank].get(eng, 0), ck)

        def slot(c, goff=0):
            return pool[c // 4][:, (c % 4) * 256 + goff:(c % 4) * 256 + goff + 128]

        def quant(src_t, src_off, W, dst_t, dst_off, src_ck=None, dst_free_ck=None,
                  hook_after_y=None):
            NB = W // 16
            s3 = lambda t, off: t[:, off:off + W].rearrange("p (b k) -> p b k", k=16)
            if src_ck is not None:
                sch.wait("vector", src_ck[0], src_ck[1])
            sch.rdrain("vector")
            sch.on("vector", lambda v, a=s3(src_t, src_off):
                   v.tensor_reduce(out=bmax[:, :NB], in_=a, axis=AXX, op=ALU.max,
                                   apply_absolute_value=True))
            sch.rdrain("vector")
            sch.on("vector", lambda v: v.tensor_scalar(
                out=sblk[:, :NB], in0=bmax[:, :NB], scalar1=1.0 / 6.0, scalar2=1e-12,
                op0=ALU.mult, op1=ALU.max))
            sch.rdrain("vector")
            sch.on("vector", lambda v: v.reciprocal(out=rblk[:, :NB], in_=sblk[:, :NB]))
            sch.rdrain("vector")
            rb3 = rblk[:, :NB].unsqueeze(2).broadcast_to([128, NB, 16])
            sch.on("vector", lambda v, a=s3(src_t, src_off), r=rb3, o=s3(tmpY, 0):
                   v.tensor_tensor(out=o, in0=a, in1=r, op=ALU.mult))
            y_ck = sch.ck("vector")
            if hook_after_y is not None:
                hook_after_y()

            sch.wait("scalar", "vector", y_ck)
            sch.on("scalar", lambda s: s.activation(out=tmpS[:, :W], in_=tmpY[:, :W],
                                                    func=ACT.Sign))
            sch.sdrain("scalar")
            sch.on("scalar", lambda s: s.activation(out=tmpY[:, :W], in_=tmpY[:, :W],
                                                    func=ACT.Abs))
            cks = sch.ck("scalar")

            sch.wait("vector", "scalar", cks)
            if dst_free_ck is not None:
                sch.wait("vector", dst_free_ck[0], dst_free_ck[1])
            B1 = tmpB[:, :W]
            B2 = dst_t[:, dst_off:dst_off + W]
            YY = tmpY[:, :W]
            sch.on("vector", lambda v: v.tensor_scalar(out=B2, in0=YY, scalar1=0.25,
                                                       scalar2=0.5, op0=ALU.is_gt,
                                                       op1=ALU.mult))
            for srcap, th, w in ((YY, 0.75, 0.5), (YY, 1.25, 0.5), (YY, 1.75, 0.5),
                                 (YY, 2.5, 1.0), (YY, 3.5, 1.0), (YY, 5.0, 2.0)):
                sch.on("vector", lambda v, srcap=srcap, th=th, w=w: v.tensor_scalar(
                    out=B1, in0=srcap, scalar1=th, scalar2=w, op0=ALU.is_gt,
                    op1=ALU.mult))
                sch.sdrain("vector")
                sch.on("vector", lambda v: v.tensor_tensor(out=B2, in0=B2, in1=B1,
                                                           op=ALU.add))
                sch.sdrain("vector")
            ckv2 = sch.ck("vector")

            sch.on("vector", lambda v: v.tensor_tensor(
                out=tmpB[:, :W], in0=dst_t[:, dst_off:dst_off + W],
                in1=tmpS[:, :W], op=ALU.mult))
            sch.sdrain("vector")
            sb3 = sblk[:, :NB].unsqueeze(2).broadcast_to([128, NB, 16])
            sch.on("vector", lambda v, sc=sb3, o=s3(dst_t, dst_off):
                   v.tensor_tensor(out=o, in0=s3(tmpB, 0), in1=sc, op=ALU.mult))
            q_ck = sch.ck("vector")
            st_["last_q_g"] = q_ck
            return y_ck, q_ck

        def ln_stats(src_t, src_off, W, prod_ck):
            if prod_ck is not None:
                sch.wait("vector", prod_ck[0], prod_ck[1])
            sch.wait("vector", "scalar", st_["last_ln_s"])
            sch.rdrain("vector")
            sch.on("vector", lambda v: v.bn_stats(out=bn6[:, 0:6],
                                                  in_=src_t[:, src_off:src_off + 512]))
            sch.on("vector", lambda v: v.bn_stats(out=bn6[:, 6:12],
                                                  in_=src_t[:, src_off + 512:src_off + W]))
            sch.rdrain("vector")
            sch.on("vector", lambda v: v.bn_aggr(out=mv[:], in_=bn6[:]))
            sch.rdrain("vector")
            sch.on("vector", lambda v: v.tensor_scalar(out=negm[:], in0=mv[:, 0:1],
                                                       scalar1=-1.0, scalar2=None,
                                                       op0=ALU.mult))
            sch.on("vector", lambda v: v.tensor_scalar(out=nmi[:], in0=mv[:, 1:2],
                                                       scalar1=1e-5, scalar2=None,
                                                       op0=ALU.add))
            sch.rdrain("vector")
            sch.on("vector", lambda v: v.reciprocal(out=mv[:, 1:2], in_=nmi[:]))
            sch.rdrain("vector")
            sch.on("vector", lambda v: v.engine_nop())
            ckv = sch.ck("vector")
            sch.wait("scalar", "vector", ckv)
            sch.on("scalar", lambda s: s.activation(out=inv[:], in_=mv[:, 1:2],
                                                    func=ACT.Sqrt))
            sck = sch.ck("scalar")
            st_["last_ln_s"] = sck
            sch.wait("vector", "scalar", sck)
            sch.on("vector", lambda v: v.tensor_scalar(
                out=nmi[:], in0=negm[:], scalar1=inv[:, 0:1], scalar2=None,
                op0=ALU.mult))
            sch.sdrain("vector")
            return sck

        with nc.sbuf_tensor("iot", [128, 128], BF16) as iot, \
             nc.sbuf_tensor("iop", [128, 128], BF16) as iop:
            wst = xst

            sch.on("gpsimd", lambda g: g.iota(iot[:], [[1, 128]], channel_multiplier=0,
                                              allow_small_or_imprecise_dtypes=True))
            sch.on("gpsimd", lambda g: g.iota(iop[:], [[0, 128]], channel_multiplier=1,
                                              allow_small_or_imprecise_dtypes=True))
            ckg0 = sch.ck("gpsimd")
            sch.wait("vector", "gpsimd", ckg0)
            sch.on("vector", lambda v: v.tensor_tensor(out=idn[:], in0=iot[:], in1=iop[:],
                                                       op=ALU.is_equal))
            sch.on("vector", lambda v: v.memset(eps_sb[:], 1e-5))
            ck_idn = sch.ck("vector")
            st_["pool_extra"] = ("vector", ck_idn)

            vA = sch.dma_on("sync", "A", lambda e, sem: e.dma_start(
                out=a_sb[:], in_=a_h.rearrange("(c p) -> p c", p=128),
                allow_slow_non_contiguous=True).then_inc(sem, 16))

            wst_free = [None, None]
            bc_cast = {}
            for c in range(DCH):
                if wst_free[c % 2] is not None:
                    sch.wait("sync", wst_free[c % 2][0], wst_free[c % 2][1])
                vL = sch.dma_on("sync", f"s{c % 2}", lambda e, sem, c=c: e.dma_start(
                    out=wst[c % 2][:, :1024],
                    in_=bc_h[c * 128:(c + 1) * 128, :]).then_inc(sem, 16))
                sch.waitd("scalar", f"s{c % 2}", vL)
                sch.on("scalar", lambda s, c=c: s.copy(out=bcq[:, c * D:(c + 1) * D],
                                                       in_=wst[c % 2][:, :1024]))
                bc_cast[c] = sch.ck("scalar")
                wst_free[c % 2] = ("scalar", bc_cast[c])
            ck_bc = bc_cast[DCH - 1]

            hqb_pe = {"ck": 0}

            def prep_mat(src_fn, nrc, rowlen, dst_t, colbase, colstep):
                ntc = rowlen // 128
                for rc in range(nrc):
                    stg = wst[rc % 2]
                    if wst_free[rc % 2] is not None:
                        sch.wait("sync", wst_free[rc % 2][0], wst_free[rc % 2][1])
                    vL = sch.dma_on("sync", f"s{rc % 2}", lambda e, sem, stg=stg, rc=rc:
                                    e.dma_start(out=stg[:, :rowlen],
                                                in_=src_fn(rc)).then_inc(sem, 16))
                    sch.waitd("vector", f"s{rc % 2}", vL)
                    y_ck, q_ck = quant(stg, 0, rowlen, hqb, 0,
                                       dst_free_ck=("tensor", hqb_pe["ck"]))
                    wst_free[rc % 2] = ("vector", y_ck)
                    for tc0 in range(0, ntc, 8):
                        nt_b = min(8, ntc - tc0)
                        banks = sorted({k // 4 for k in range(nt_b)})
                        pool_wait(banks)
                        if tc0 == 0:
                            sch.wait("tensor", "vector", q_ck)
                        for k in range(nt_b):
                            sch.on("tensor", lambda t, tc=tc0 + k, k=k: t.transpose(
                                slot(k), hqb[:, tc * 128:(tc + 1) * 128], idn[:]))
                        ckp = sch.ck("tensor")
                        hqb_pe["ck"] = ckp
                        sch.wait("scalar", "tensor", ckp)
                        for b in banks:
                            nb = min(4, nt_b - b * 4)
                            inap = pool[b][:].rearrange("p (n x) -> p n x",
                                                        x=256)[:, 0:nb, 0:128]
                            n0 = colbase // colstep + tc0 + b * 4
                            outap = dst_t[:].rearrange("p (n x) -> p n x",
                                                       x=colstep)[:, n0:n0 + nb,
                                                                  rc * 128:rc * 128 + 128]
                            sch.on("scalar", lambda s, i=inap, o=outap: s.copy(
                                out=o, in_=i))
                        cke = sch.ck("scalar")
                        for b in banks:
                            pool_consumed(b, "scalar", cke)

            prep_mat(lambda rc: wg_h[rc * 128:(rc + 1) * 128, :], ICH, 1024,
                     wgT, 0, I)
            prep_mat(lambda rc: wu_h[rc * 128:(rc + 1) * 128, :], ICH, 1024,
                     wuT, 0, I)
            for half in range(2):
                prep_mat(lambda rc, h=half: wd_h[rc * 128:(rc + 1) * 128,
                                                 h * IH:(h + 1) * IH], DCH, IH,
                         wdT, half * 11 * D, D)

        prep_done = {e: sch.cnt[e] for e in ENGS}

        stg_readers = {}
        hs_attn_ck = {}
        attn_free = {"ck": 0}
        gu_mult_ck = {}
        xn2_y = {"ck": 0}
        hbuf_y = {"ck": 0}
        xn_pe = {"ck": 0}
        pdown_free = {"ck": 0}
        out_dma = [0, 0]
        dma_j = {}
        xqf_ck = {}

        def emit_xqT(t, g, q_ck):
            pool_wait([0, 1])
            sch.wait("tensor", "vector", q_ck)
            if t == 0 and g == 0:
                sch.wait("tensor", "scalar", prep_done["scalar"])
            for c in range(DCH):
                sch.on("tensor", lambda tt, c=c, g=g: tt.transpose(
                    slot(c, g * 128), xq_tm[:, c * 128:(c + 1) * 128], idn[:]))
            ck_xqT = sch.ck("tensor")
            hqb_pe["ck"] = ck_xqT
            sch.wait("scalar", "tensor", ck_xqT)
            for b in range(2):
                inap = pool[b][:].rearrange("p (n x) -> p n x",
                                            x=256)[:, 0:4, g * 128:g * 128 + 128]
                outap = xqf[:].rearrange("p (n x) -> p n x",
                                         x=T)[:, b * 4:b * 4 + 4,
                                              g * 128:g * 128 + 128]
                sch.on("scalar", lambda s, i=inap, o=outap: s.copy(out=o, in_=i))
            ck = sch.ck("scalar")
            xqf_ck[g] = ck
            pool_consumed(0, "scalar", ck)
            pool_consumed(1, "scalar", ck)

        def emit_loads(tt):
            if tt >= NT:
                return
            sq, kq = divmod(tt, KPS)
            r00 = sq * S + kq * T
            for g in range(NG):
                j = NG * tt + g
                stg = xst[j % 2]
                if j >= 2:
                    for (e, v) in stg_readers[j - 2]:
                        sch.wait("sync", e, v)
                elif wst_free[j % 2] is not None:
                    sch.wait("sync", wst_free[j % 2][0], wst_free[j % 2][1])
                dma_j[j] = sch.dma_on("sync", f"s{j % 2}", lambda e, sem, stg=stg,
                                      r0=r00 + g * 128: e.dma_start(
                    out=stg[:, :D], in_=x_h[r0:r0 + 128, :]).then_inc(sem, 16))

        emit_loads(0)

        for t in range(NT):
            seq, kk = divmod(t, KPS)
            row0 = seq * S + kk * T

            tck = {}
            for g in range(NG):
                j = NG * t + g
                sch.waitd("vector", f"s{j % 2}", dma_j[j])
                if t == 0 and g == 0:
                    sch.wait("vector", "gpsimd", prep_done["gpsimd"])
                ln_stats(xst[j % 2], 0, D, None)
                sch.wait("vector", "tensor", xn_pe["ck"])
                sch.on("vector", lambda v, stg=xst[j % 2]: v.scalar_tensor_tensor(
                    out=xn[:], in0=stg[:, :D], scalar=inv[:, 0:1],
                    in1=nmi[:].broadcast_to([128, D]), op0=ALU.mult, op1=ALU.add))
                ck_xn = sch.ck("vector")
                stg_readers[j] = [("vector", ck_xn)]
                if g == 0:
                    pool_wait([0, 1])
                sch.wait("tensor", "vector", ck_xn)
                for c in range(DCH):
                    sch.on("tensor", lambda tt, c=c, g=g: tt.transpose(
                        slot(c, g * 128), xn[:, c * 128:(c + 1) * 128], idn[:]))
                tck[g] = sch.ck("tensor")
                xn_pe["ck"] = tck[g]

            sch.wait("vector", "tensor", tck[1])
            if t >= 1:
                sch.wait("vector", "tensor", hs_attn_ck[t - 1])
            if t == 0:
                sch.waitd("vector", "A", vA)
            sch.rdrain("vector")
            for c in range(DCH):
                init = 0.0 if kk == 0 else carry[:, c:c + 1]
                a0 = a_sb[:, c:c + 1].broadcast_to([128, 256])
                psrc = pool[c // 4][:].rearrange("p (n x) -> p n x",
                                                 x=256)[:, c % 4, :]
                sch.on("vector", lambda v, c=c, init=init, a0=a0, psrc=psrc:
                       v.tensor_tensor_scan(out=hs[:, c * T:c * T + 256],
                                            data0=a0, data1=psrc, initial=init,
                                            op0=ALU.mult, op1=ALU.add))
            sch.rdrain("vector")
            sch.on("vector", lambda v: v.tensor_copy(
                out=carry[:].unsqueeze(2),
                in_=hs[:].rearrange("p (c t) -> p c t", t=T)[:, :, T - 1:T]))
            ck_scan = sch.ck("vector")
            pool_consumed(0, "vector", ck_scan)
            pool_consumed(1, "vector", ck_scan)

            radd_ck = {}
            for g in range(NG):
                j = NG * t + g
                sch.wait("tensor", "vector", ck_scan)
                if t == 0 and g == 0:
                    sch.wait("tensor", "scalar", ck_bc)
                sch.wait("tensor", "vector", attn_free["ck"])
                for os in range(2):
                    for c in range(DCH):
                        sch.on("tensor", lambda tt, os=os, c=c, g=g: tt.matmul(
                            pattn[os][:], hs[:, c * T + g * 128:c * T + (g + 1) * 128],
                            bcq[:, c * D + os * 512:c * D + os * 512 + 512],
                            start=(c == 0), stop=(c == DCH - 1)))
                ck_attn = sch.ck("tensor")
                if g == NG - 1:
                    hs_attn_ck[t] = ck_attn

                sch.wait("vector", "tensor", ck_attn)
                sch.waitd("vector", f"o{g}", out_dma[g])
                for os in range(2):
                    sch.on("vector", lambda v, os=os, g=g, stg=xst[j % 2]:
                           v.tensor_tensor(out=x1f[g][:, os * 512:os * 512 + 512],
                                           in0=stg[:, os * 512:os * 512 + 512],
                                           in1=pattn[os][:], op=ALU.add))
                radd_ck[g] = sch.ck("vector")
                attn_free["ck"] = radd_ck[g]
                stg_readers[j].append(("vector", radd_ck[g]))

            emit_loads(t + 1)

            qck = {}
            xn2ck = {}

            def emit_ln2(g):
                ln_stats(x1f[g], 0, D, ("vector", radd_ck[g]))
                sch.on("vector", lambda v, g=g: v.scalar_tensor_tensor(
                    out=xn2[:], in0=x1f[g][:], scalar=inv[:, 0:1],
                    in1=nmi[:].broadcast_to([128, D]), op0=ALU.mult, op1=ALU.add))
                xn2ck[g] = sch.ck("vector")

            emit_ln2(0)
            y_ck, qck[0] = quant(xn2, 0, D, xq_tm, 0,
                                 src_ck=("vector", xn2ck[0]),
                                 dst_free_ck=("tensor", hqb_pe["ck"]),
                                 hook_after_y=lambda: emit_ln2(1))
            emit_xqT(t, 0, qck[0])
            for g in range(1, NG):
                y_ck, qck[g] = quant(xn2, 0, D, xq_tm, 0,
                                     src_ck=("vector", xn2ck[g]),
                                     dst_free_ck=("tensor", hqb_pe["ck"]))

            def gu_slices(g, mid_hook=None):
                for si, (soff, sw) in enumerate(SL512):
                    pg = (pgu, pattn, pdown)[si % 3]
                    sch.wait("tensor", "scalar", xqf_ck[g])
                    prevmult = gu_mult_ck.get((g, si - 3)) if si >= 3 else (
                        gu_mult_ck.get((1 - g, NSL - 3 + si)))
                    if prevmult:
                        sch.wait("tensor", "vector", prevmult)
                    if si in (0, 1):
                        sch.wait("tensor", "vector", attn_free["ck"])
                    if si == 2:
                        sch.wait("tensor", "vector", pdown_free["ck"])
                    for c in range(DCH):
                        sch.on("tensor", lambda tt, c=c, g=g, soff=soff, sw=sw,
                               pg=pg: tt.matmul(
                            pg[0][:, :sw], xqf[:, c * T + g * 128:c * T + (g + 1) * 128],
                            wgT[:, c * I + soff:c * I + soff + sw],
                            start=(c == 0), stop=(c == DCH - 1)))
                    for c in range(DCH):
                        sch.on("tensor", lambda tt, c=c, g=g, soff=soff, sw=sw,
                               pg=pg: tt.matmul(
                            pg[1][:, :sw], xqf[:, c * T + g * 128:c * T + (g + 1) * 128],
                            wuT[:, c * I + soff:c * I + soff + sw],
                            start=(c == 0), stop=(c == DCH - 1)))
                    ck_gu = sch.ck("tensor")
                    sch.wait("scalar", "tensor", ck_gu)
                    prevmult1 = gu_mult_ck.get((g, si - 1)) if si >= 1 else (
                        gu_mult_ck.get((1 - g, NSL - 1)))
                    if prevmult1:
                        sch.wait("scalar", "vector", prevmult1)
                    sch.on("scalar", lambda s, sw=sw, pg=pg: s.activation(
                        out=sil[:, :sw], in_=pg[0][:, :sw], func=ACT.Silu))
                    ck_sil = sch.ck("scalar")
                    sch.wait("vector", "scalar", ck_sil)
                    sch.on("vector", lambda v, g=g, soff=soff, sw=sw, pg=pg:
                           v.tensor_tensor(
                        out=hbuf2[g][:, soff:soff + sw], in0=sil[:, :sw],
                        in1=pg[1][:, :sw], op=ALU.mult))
                    gu_mult_ck[(g, si)] = sch.ck("vector")
                    if si == 3 and mid_hook is not None:
                        mid_hook()
                attn_free["ck"] = max(attn_free["ck"], gu_mult_ck[(g, NSL - 1)])
                pdown_free["ck"] = max(pdown_free["ck"], gu_mult_ck[(g, NSL - 1)])

            gu_slices(0)
            emit_xqT(t, 1, qck[1])
            hq_pre = {}

            def _pre_hq0():
                y_ck, q_ck = quant(hbuf2[0], 0, IH, hqb, 0,
                                   src_ck=("vector", gu_mult_ck[(0, NSL - 1)]),
                                   dst_free_ck=("tensor", hqb_pe["ck"]))
                hq_pre["q"] = q_ck

            gu_slices(1, mid_hook=_pre_hq0)

            def emit_out(g, dck):
                sch.wait("vector", "tensor", dck)
                for dsl in range(2):
                    sch.on("vector", lambda v, g=g, dsl=dsl: v.tensor_tensor(
                        out=x1f[g][:, dsl * 512:dsl * 512 + 512],
                        in0=x1f[g][:, dsl * 512:dsl * 512 + 512],
                        in1=pdown[dsl][:], op=ALU.add))
                eck = sch.ck("vector")
                pdown_free["ck"] = max(pdown_free["ck"], eck)
                sch.wait("gpsimd", "vector", eck)
                r0g = row0 + g * 128
                out_dma[g] = sch.dma_on("gpsimd", f"o{g}", lambda e, sem,
                                        r0g=r0g, g=g: e.dma_start(
                    out=y_h[r0g:r0g + 128, :], in_=x1f[g][:, :D]).then_inc(sem, 16))

            down_cks = {}
            for g in range(NG):
                j = NG * t + g
                down_ck = 0
                for half in range(2):
                    if g == 0 and half == 0:
                        hq_ck = hq_pre["q"]
                    else:
                        hook = None
                        if g == 1 and half == 0:
                            hook = lambda: emit_out(0, down_cks[0])
                        y_ck, hq_ck = quant(hbuf2[g], half * IH, IH, hqb, 0,
                                            src_ck=("vector",
                                                    gu_mult_ck[(g, NSL - 1)]),
                                            dst_free_ck=("tensor", hqb_pe["ck"]),
                                            hook_after_y=hook)
                    for bi, nb in enumerate((4, 4, 3)):
                        bank = bi % 2
                        pool_wait([bank])
                        if bi == 0:
                            sch.wait("tensor", "vector", hq_ck)
                        for k in range(nb):
                            sch.on("tensor", lambda tt, bi=bi, k=k, bank=bank:
                                   tt.transpose(pool[bank][:, k * 256:k * 256 + 128],
                                                hqb[:, (bi * 4 + k) * 128:
                                                    (bi * 4 + k) * 128 + 128], idn[:]))
                        ck_hqT = sch.ck("tensor")
                        hqb_pe["ck"] = ck_hqT
                        sch.wait("scalar", "tensor", ck_hqT)
                        if down_ck:
                            sch.wait("scalar", "tensor", down_ck)
                        inap = pool[bank][:].rearrange("p (n x) -> p n x",
                                                       x=256)[:, 0:nb, 0:128]
                        outap = hqf[:].rearrange("p (n x) -> p n x",
                                                 x=128)[:, 0:nb, :]
                        sch.on("scalar", lambda s, i=inap, o=outap: s.copy(out=o, in_=i))
                        ck_ev = sch.ck("scalar")
                        pool_consumed(bank, "scalar", ck_ev)
                        sch.wait("tensor", "scalar", ck_ev)
                        first = (half == 0 and bi == 0)
                        last = (half == 1 and bi == 2)
                        if first:
                            sch.wait("tensor", "vector", pdown_free["ck"])
                        for k in range(nb):
                            ic = half * 11 + bi * 4 + k
                            for dsl in range(2):
                                sch.on("tensor", lambda tt, k=k, ic=ic, dsl=dsl,
                                       st=(first and k == 0),
                                       sp=(last and k == nb - 1): tt.matmul(
                                    pdown[dsl][:], hqf[:, k * 128:k * 128 + 128],
                                    wdT[:, ic * D + dsl * 512:ic * D + dsl * 512 + 512],
                                    start=st, stop=sp))
                        down_ck = sch.ck("tensor")
                down_cks[g] = down_ck
            emit_out(1, down_cks[1])

        sch.on("sync", lambda e, t0=sch.dma["o0"]: e.wait_ge(sems["dO0"], t0))
        sch.on("sync", lambda e, t1=sch.dma["o1"]: e.wait_ge(sems["dO1"], t1))

        with nc.Block() as block:
            sim_drains = SIM_DRAINS

            def run(e, eng):
                last = [None]
                my = ESEM[eng]
                for item in sch.ops[eng]:
                    kind = item[0]
                    if kind == "op":
                        r = item[1](e)
                        if r is not None:
                            last[0] = r
                    elif kind == "wait":
                        e.wait_ge(sems[ESEM[item[1]]], item[2])
                    elif kind == "waitd":
                        e.wait_ge(sems[DSEM[item[1]]], item[2])
                    elif kind == "sdrain":
                        if sim_drains and eng != "sync":
                            e.drain()
                    elif kind == "rdrain":
                        e.drain()
                    elif kind == "ck":
                        if eng == "sync":
                            e.nop().then_inc(sems[my], 1)
                        elif last[0] is not None:
                            last[0].then_inc(sems[my], 1)
                            last[0] = None
                        else:
                            e.drain().then_inc(sems[my], 1)
                    elif kind == "dma":
                        item[2](e, sems[DSEM[item[1]]])

            @block.sync
            def _(e):
                run(e, "sync")

            @block.vector
            def _(e):
                run(e, "vector")

            @block.scalar
            def _(e):
                run(e, "scalar")

            @block.tensor
            def _(e):
                run(e, "tensor")

            @block.gpsimd
            def _(e):
                run(e, "gpsimd")

    return nc


_NC_CACHE = {}


def kernel(L0_attn_inputs, L0_attn_A, L0_attn_BC, L0_ffn_w_gate, L0_ffn_w_up,
           L0_ffn_w_down):
    x = np.ascontiguousarray(np.asarray(L0_attn_inputs, dtype=np.float32))
    A = np.ascontiguousarray(np.asarray(L0_attn_A, dtype=np.float32))
    BC = np.ascontiguousarray(np.asarray(L0_attn_BC, dtype=np.float32))
    WG = np.ascontiguousarray(np.asarray(L0_ffn_w_gate, dtype=np.float32))
    WU = np.ascontiguousarray(np.asarray(L0_ffn_w_up, dtype=np.float32))
    WD = np.ascontiguousarray(np.asarray(L0_ffn_w_down, dtype=np.float32))

    if "nc" not in _NC_CACHE:
        _NC_CACHE["nc"] = build_program()
    nc = _NC_CACHE["nc"]

    in_maps = []
    for c in range(NCORES):
        xs = np.ascontiguousarray(x[c * BPC:(c + 1) * BPC].reshape(BPC * S, D))
        in_maps.append({"x": xs, "A": A, "BC": BC, "WG": WG, "WU": WU, "WD": WD})

    res = run_bass_kernel_spmd(nc, in_maps, core_ids=list(range(NCORES)))
    out = np.concatenate([r["y"].reshape(BPC, S, D) for r in res.results], axis=0)
    return out


# version 24 (final)
# speedup vs baseline: 1.0295x; 1.0090x over previous
import os
import numpy as np
import concourse.bass as bass
import concourse.mybir as mybir
from concourse.bass_utils import run_bass_kernel_spmd

F32 = mybir.dt.float32
BF16 = mybir.dt.bfloat16
F16 = mybir.dt.float16
ALU = mybir.AluOpType
ACT = mybir.ActivationFunctionType
AXX = mybir.AxisListType.X

B, S, D, I = 32, 2048, 1024, 2816
NCORES = 8
BPC = B // NCORES
T = 256
NG = T // 128
KPS = S // T
NT = BPC * KPS
DCH = D // 128
ICH = I // 128
IH = I // 2
NBH = IH // 16
SL512 = [(0, 512), (512, 512), (1024, 512), (1536, 512), (2048, 512), (2560, 256)]
NSL = len(SL512)

ENGS = ("sync", "vector", "scalar", "tensor", "gpsimd")

SIM_DRAINS = os.environ.get("BASS_SIM_FP64") == "1"
KDUMP = os.environ.get("KDUMP", "")


class Sch:

    def __init__(self):
        self.ops = {e: [] for e in ENGS}
        self.cnt = {e: 0 for e in ENGS}
        self.dma = {"s0": 0, "s1": 0, "A": 0, "o0": 0, "o1": 0}
        self.hw = {}

    def on(self, eng, fn):
        self.ops[eng].append(("op", fn))

    def wait(self, eng, prod_eng, val):
        if eng == prod_eng:
            return
        if val and val > 0:
            k = (eng, "e" + prod_eng)
            if val <= self.hw.get(k, 0):
                return
            self.hw[k] = val
            self.ops[eng].append(("wait", prod_eng, val))

    def waitd(self, eng, which, val):
        if val and val > 0:
            k = (eng, "d" + which)
            if val <= self.hw.get(k, 0):
                return
            self.hw[k] = val
            self.ops[eng].append(("waitd", which, val))

    def ck(self, eng):
        self.cnt[eng] += 1
        self.ops[eng].append(("ck", self.cnt[eng]))
        if eng in ("vector", "scalar", "gpsimd"):
            self.ops[eng].append(("sdrain",))
        return self.cnt[eng]

    def sdrain(self, eng):
        self.ops[eng].append(("sdrain",))

    def rdrain(self, eng):
        self.ops[eng].append(("rdrain",))

    def dma_on(self, eng, which, fn):
        self.dma[which] += 16
        self.ops[eng].append(("dma", which, fn))
        return self.dma[which]


def build_program():
    nc = bass.Bass("TRN2", target_bir_lowering=False)
    x_h = nc.dram_tensor("x", [BPC * S, D], F32, kind="ExternalInput").ap()
    a_h = nc.dram_tensor("A", [D], F32, kind="ExternalInput").ap()
    bc_h = nc.dram_tensor("BC", [D, D], F32, kind="ExternalInput").ap()
    wg_h = nc.dram_tensor("WG", [I, D], F32, kind="ExternalInput").ap()
    wu_h = nc.dram_tensor("WU", [I, D], F32, kind="ExternalInput").ap()
    wd_h = nc.dram_tensor("WD", [D, I], F32, kind="ExternalInput").ap()
    y_h = nc.dram_tensor("y", [BPC * S, D], F32, kind="ExternalOutput").ap()

    sch = Sch()
    from contextlib import ExitStack
    with ExitStack() as es:
        sb = lambda n, sh, dt: es.enter_context(nc.sbuf_tensor(n, sh, dt))
        ps = lambda n, sh: es.enter_context(nc.psum_tensor(n, sh, F32))

        wgT = sb("wgT", [128, DCH * I], BF16)
        wuT = sb("wuT", [128, DCH * I], BF16)
        wdT = sb("wdT", [128, ICH * D], BF16)
        bcq = sb("bcq", [128, DCH * D], BF16)
        a_sb = sb("a_sb", [128, DCH], F32)
        idn = sb("idn", [128, 128], BF16)
        eps_sb = sb("eps_sb", [128, 1], F32)

        xst = [sb(f"xst{i}", [128, IH], F32) for i in range(2)]
        x1f = [sb(f"x1f{i}", [128, D], F32) for i in range(2)]
        xn2 = sb("xn2", [128, D], F16)
        hs = sb("hs", [128, DCH * T], BF16)
        carry = sb("carry", [128, DCH], F32)
        xqf = sb("xqf", [128, DCH * T], BF16)
        hbuf2 = [sb(f"hbuf{i}", [128, I], BF16) for i in range(2)]
        hqb = sb("hqb", [128, IH], BF16)
        xq_tm = hqb
        hqf = sb("hqf", [128, 512], BF16)
        sil = sb("sil", [128, 512], BF16)
        tmpY = sb("tmpY", [128, IH], F32)
        tmpS = sb("tmpS", [128, IH], BF16)
        tmpB = sb("tmpB", [128, IH], BF16)
        xn = sb("xn", [128, D], BF16)
        bmax = sb("bmax", [128, NBH], F32)
        sblk = bmax
        rblk = sb("rblk", [128, NBH], F32)
        bn6 = sb("bn6", [128, 12], F32)
        mv = sb("mv", [128, 2], F32)
        inv = sb("inv", [128, 1], F32)
        negm = sb("negm", [128, 1], F32)
        nmi = sb("nmi", [128, 1], F32)

        pool = [es.enter_context(nc.psum_tensor("pool0", [128, 1024], BF16)),
                es.enter_context(nc.psum_tensor("pool1", [128, 1024], BF16))]
        pattn = [ps("pattn0", [128, 512]), ps("pattn1", [128, 512])]
        pgu = [ps("pgu0", [128, 512]), ps("pgu1", [128, 512])]
        pdown = [ps("pdn0", [128, 512]), ps("pdn1", [128, 512])]

        sems = {}
        for nsem in ("sV", "sS", "sP", "sG", "sSY", "dS0", "dS1", "dA", "dO0", "dO1"):
            sems[nsem] = es.enter_context(nc.semaphore(nsem))
        ESEM = {"vector": "sV", "scalar": "sS", "tensor": "sP", "gpsimd": "sG",
                "sync": "sSY"}
        DSEM = {"s0": "dS0", "s1": "dS1", "A": "dA", "o0": "dO0", "o1": "dO1"}

        st_ = {
            "pool": [None, None],
            "pool_extra": None,
            "last_q_g": 0,
            "last_q_v": 0,
            "last_ln_s": 0,
        }

        pool_users = {0: {}, 1: {}}

        def pool_wait(banks):
            for b in banks:
                for e, v in pool_users[b].items():
                    sch.wait("tensor", e, v)
                pool_users[b] = {}
            if st_["pool_extra"] is not None:
                e, v = st_["pool_extra"]
                sch.wait("tensor", e, v)
                st_["pool_extra"] = None

        def pool_consumed(bank, eng, ck):
            pool_users[bank][eng] = max(pool_users[bank].get(eng, 0), ck)

        def slot(c, goff=0):
            return pool[c // 4][:, (c % 4) * 256 + goff:(c % 4) * 256 + goff + 128]

        def quant(src_t, src_off, W, dst_t, dst_off, src_ck=None, dst_free_ck=None,
                  hook_after_y=None):
            NB = W // 16
            s3 = lambda t, off: t[:, off:off + W].rearrange("p (b k) -> p b k", k=16)
            if src_ck is not None:
                sch.wait("vector", src_ck[0], src_ck[1])
            sch.rdrain("vector")
            sch.on("vector", lambda v, a=s3(src_t, src_off):
                   v.tensor_reduce(out=bmax[:, :NB], in_=a, axis=AXX, op=ALU.max,
                                   apply_absolute_value=True))
            sch.rdrain("vector")
            sch.on("vector", lambda v: v.tensor_scalar(
                out=sblk[:, :NB], in0=bmax[:, :NB], scalar1=1.0 / 6.0, scalar2=1e-12,
                op0=ALU.mult, op1=ALU.max))
            sch.rdrain("vector")
            sch.on("vector", lambda v: v.reciprocal(out=rblk[:, :NB], in_=sblk[:, :NB]))
            sch.rdrain("vector")
            rb3 = rblk[:, :NB].unsqueeze(2).broadcast_to([128, NB, 16])
            sch.on("vector", lambda v, a=s3(src_t, src_off), r=rb3, o=s3(tmpY, 0):
                   v.tensor_tensor(out=o, in0=a, in1=r, op=ALU.mult))
            y_ck = sch.ck("vector")
            if hook_after_y is not None:
                hook_after_y()

            sch.wait("scalar", "vector", y_ck)
            sch.on("scalar", lambda s: s.activation(out=tmpS[:, :W], in_=tmpY[:, :W],
                                                    func=ACT.Sign))
            sch.sdrain("scalar")
            sch.on("scalar", lambda s: s.activation(out=tmpY[:, :W], in_=tmpY[:, :W],
                                                    func=ACT.Abs))
            cks = sch.ck("scalar")

            sch.wait("vector", "scalar", cks)
            if dst_free_ck is not None:
                sch.wait("vector", dst_free_ck[0], dst_free_ck[1])
            B1 = tmpB[:, :W]
            B2 = dst_t[:, dst_off:dst_off + W]
            YY = tmpY[:, :W]
            sch.on("vector", lambda v: v.tensor_scalar(out=B2, in0=YY, scalar1=0.25,
                                                       scalar2=0.5, op0=ALU.is_gt,
                                                       op1=ALU.mult))
            for srcap, th, w in ((YY, 0.75, 0.5), (YY, 1.25, 0.5), (YY, 1.75, 0.5),
                                 (YY, 2.5, 1.0), (YY, 3.5, 1.0), (YY, 5.0, 2.0)):
                sch.on("vector", lambda v, srcap=srcap, th=th, w=w: v.tensor_scalar(
                    out=B1, in0=srcap, scalar1=th, scalar2=w, op0=ALU.is_gt,
                    op1=ALU.mult))
                sch.sdrain("vector")
                sch.on("vector", lambda v: v.tensor_tensor(out=B2, in0=B2, in1=B1,
                                                           op=ALU.add))
                sch.sdrain("vector")
            ckv2 = sch.ck("vector")

            sch.on("vector", lambda v: v.tensor_tensor(
                out=tmpB[:, :W], in0=dst_t[:, dst_off:dst_off + W],
                in1=tmpS[:, :W], op=ALU.mult))
            sch.sdrain("vector")
            sb3 = sblk[:, :NB].unsqueeze(2).broadcast_to([128, NB, 16])
            sch.on("vector", lambda v, sc=sb3, o=s3(dst_t, dst_off):
                   v.tensor_tensor(out=o, in0=s3(tmpB, 0), in1=sc, op=ALU.mult))
            q_ck = sch.ck("vector")
            st_["last_q_g"] = q_ck
            return y_ck, q_ck

        def ln_stats(src_t, src_off, W, prod_ck):
            if prod_ck is not None:
                sch.wait("vector", prod_ck[0], prod_ck[1])
            sch.wait("vector", "scalar", st_["last_ln_s"])
            sch.rdrain("vector")
            sch.on("vector", lambda v: v.bn_stats(out=bn6[:, 0:6],
                                                  in_=src_t[:, src_off:src_off + 512]))
            sch.on("vector", lambda v: v.bn_stats(out=bn6[:, 6:12],
                                                  in_=src_t[:, src_off + 512:src_off + W]))
            sch.rdrain("vector")
            sch.on("vector", lambda v: v.bn_aggr(out=mv[:], in_=bn6[:]))
            sch.rdrain("vector")
            sch.on("vector", lambda v: v.tensor_scalar(out=negm[:], in0=mv[:, 0:1],
                                                       scalar1=-1.0, scalar2=None,
                                                       op0=ALU.mult))
            sch.on("vector", lambda v: v.tensor_scalar(out=nmi[:], in0=mv[:, 1:2],
                                                       scalar1=1e-5, scalar2=None,
                                                       op0=ALU.add))
            sch.rdrain("vector")
            sch.on("vector", lambda v: v.reciprocal(out=mv[:, 1:2], in_=nmi[:]))
            sch.rdrain("vector")
            sch.on("vector", lambda v: v.engine_nop())
            ckv = sch.ck("vector")
            sch.wait("scalar", "vector", ckv)
            sch.on("scalar", lambda s: s.activation(out=inv[:], in_=mv[:, 1:2],
                                                    func=ACT.Sqrt))
            sck = sch.ck("scalar")
            st_["last_ln_s"] = sck
            sch.wait("vector", "scalar", sck)
            sch.on("vector", lambda v: v.tensor_scalar(
                out=nmi[:], in0=negm[:], scalar1=inv[:, 0:1], scalar2=None,
                op0=ALU.mult))
            sch.sdrain("vector")
            return sck

        with nc.sbuf_tensor("iot", [128, 128], BF16) as iot, \
             nc.sbuf_tensor("iop", [128, 128], BF16) as iop:
            wst = xst

            sch.on("gpsimd", lambda g: g.iota(iot[:], [[1, 128]], channel_multiplier=0,
                                              allow_small_or_imprecise_dtypes=True))
            sch.on("gpsimd", lambda g: g.iota(iop[:], [[0, 128]], channel_multiplier=1,
                                              allow_small_or_imprecise_dtypes=True))
            ckg0 = sch.ck("gpsimd")
            sch.wait("vector", "gpsimd", ckg0)
            sch.on("vector", lambda v: v.tensor_tensor(out=idn[:], in0=iot[:], in1=iop[:],
                                                       op=ALU.is_equal))
            sch.on("vector", lambda v: v.memset(eps_sb[:], 1e-5))
            ck_idn = sch.ck("vector")
            st_["pool_extra"] = ("vector", ck_idn)

            vA = sch.dma_on("sync", "A", lambda e, sem: e.dma_start(
                out=a_sb[:], in_=a_h.rearrange("(c p) -> p c", p=128),
                allow_slow_non_contiguous=True).then_inc(sem, 16))

            wst_free = [None, None]
            bc_cast = {}
            for c in range(DCH):
                if wst_free[c % 2] is not None:
                    sch.wait("sync", wst_free[c % 2][0], wst_free[c % 2][1])
                vL = sch.dma_on("sync", f"s{c % 2}", lambda e, sem, c=c: e.dma_start(
                    out=wst[c % 2][:, :1024],
                    in_=bc_h[c * 128:(c + 1) * 128, :]).then_inc(sem, 16))
                sch.waitd("scalar", f"s{c % 2}", vL)
                sch.on("scalar", lambda s, c=c: s.copy(out=bcq[:, c * D:(c + 1) * D],
                                                       in_=wst[c % 2][:, :1024]))
                bc_cast[c] = sch.ck("scalar")
                wst_free[c % 2] = ("scalar", bc_cast[c])
            ck_bc = bc_cast[DCH - 1]

            hqb_pe = {"ck": 0}

            def prep_mat(src_fn, nrc, rowlen, dst_t, colbase, colstep):
                ntc = rowlen // 128
                for rc in range(nrc):
                    stg = wst[rc % 2]
                    if wst_free[rc % 2] is not None:
                        sch.wait("sync", wst_free[rc % 2][0], wst_free[rc % 2][1])
                    vL = sch.dma_on("sync", f"s{rc % 2}", lambda e, sem, stg=stg, rc=rc:
                                    e.dma_start(out=stg[:, :rowlen],
                                                in_=src_fn(rc)).then_inc(sem, 16))
                    sch.waitd("vector", f"s{rc % 2}", vL)
                    y_ck, q_ck = quant(stg, 0, rowlen, hqb, 0,
                                       dst_free_ck=("tensor", hqb_pe["ck"]))
                    wst_free[rc % 2] = ("vector", y_ck)
                    for tc0 in range(0, ntc, 8):
                        nt_b = min(8, ntc - tc0)
                        banks = sorted({k // 4 for k in range(nt_b)})
                        pool_wait(banks)
                        if tc0 == 0:
                            sch.wait("tensor", "vector", q_ck)
                        for k in range(nt_b):
                            sch.on("tensor", lambda t, tc=tc0 + k, k=k: t.transpose(
                                slot(k), hqb[:, tc * 128:(tc + 1) * 128], idn[:]))
                        ckp = sch.ck("tensor")
                        hqb_pe["ck"] = ckp
                        sch.wait("scalar", "tensor", ckp)
                        for b in banks:
                            nb = min(4, nt_b - b * 4)
                            inap = pool[b][:].rearrange("p (n x) -> p n x",
                                                        x=256)[:, 0:nb, 0:128]
                            n0 = colbase // colstep + tc0 + b * 4
                            outap = dst_t[:].rearrange("p (n x) -> p n x",
                                                       x=colstep)[:, n0:n0 + nb,
                                                                  rc * 128:rc * 128 + 128]
                            sch.on("scalar", lambda s, i=inap, o=outap: s.copy(
                                out=o, in_=i))
                        cke = sch.ck("scalar")
                        for b in banks:
                            pool_consumed(b, "scalar", cke)

            prep_mat(lambda rc: wg_h[rc * 128:(rc + 1) * 128, :], ICH, 1024,
                     wgT, 0, I)
            prep_mat(lambda rc: wu_h[rc * 128:(rc + 1) * 128, :], ICH, 1024,
                     wuT, 0, I)
            for half in range(2):
                prep_mat(lambda rc, h=half: wd_h[rc * 128:(rc + 1) * 128,
                                                 h * IH:(h + 1) * IH], DCH, IH,
                         wdT, half * 11 * D, D)

        prep_done = {e: sch.cnt[e] for e in ENGS}

        stg_readers = {}
        hs_attn_ck = {}
        attn_free = {"ck": 0}
        gu_mult_ck = {}
        xn2_y = {"ck": 0}
        hbuf_y = {"ck": 0}
        xn_pe = {"ck": 0}
        pdown_free = {"ck": 0}
        out_dma = [0, 0]
        dma_j = {}
        xqf_ck = {}

        def emit_xqT(t, g, q_ck):
            pool_wait([0, 1])
            sch.wait("tensor", "vector", q_ck)
            if t == 0 and g == 0:
                sch.wait("tensor", "scalar", prep_done["scalar"])
            for c in range(DCH):
                sch.on("tensor", lambda tt, c=c, g=g: tt.transpose(
                    slot(c, g * 128), xq_tm[:, c * 128:(c + 1) * 128], idn[:]))
            ck_xqT = sch.ck("tensor")
            hqb_pe["ck"] = ck_xqT
            sch.wait("scalar", "tensor", ck_xqT)
            for b in range(2):
                inap = pool[b][:].rearrange("p (n x) -> p n x",
                                            x=256)[:, 0:4, g * 128:g * 128 + 128]
                outap = xqf[:].rearrange("p (n x) -> p n x",
                                         x=T)[:, b * 4:b * 4 + 4,
                                              g * 128:g * 128 + 128]
                sch.on("scalar", lambda s, i=inap, o=outap: s.copy(out=o, in_=i))
            ck = sch.ck("scalar")
            xqf_ck[g] = ck
            pool_consumed(0, "scalar", ck)
            pool_consumed(1, "scalar", ck)

        def emit_loads(tt):
            if tt >= NT:
                return
            sq, kq = divmod(tt, KPS)
            r00 = sq * S + kq * T
            for g in range(NG):
                j = NG * tt + g
                stg = xst[j % 2]
                if j >= 2:
                    for (e, v) in stg_readers[j - 2]:
                        sch.wait("sync", e, v)
                elif wst_free[j % 2] is not None:
                    sch.wait("sync", wst_free[j % 2][0], wst_free[j % 2][1])
                dma_j[j] = sch.dma_on("sync", f"s{j % 2}", lambda e, sem, stg=stg,
                                      r0=r00 + g * 128: e.dma_start(
                    out=stg[:, :D], in_=x_h[r0:r0 + 128, :]).then_inc(sem, 16))

        emit_loads(0)

        for t in range(NT):
            seq, kk = divmod(t, KPS)
            row0 = seq * S + kk * T

            tck = {}
            for g in range(NG):
                j = NG * t + g
                sch.waitd("vector", f"s{j % 2}", dma_j[j])
                if t == 0 and g == 0:
                    sch.wait("vector", "gpsimd", prep_done["gpsimd"])
                ln_stats(xst[j % 2], 0, D, None)
                sch.wait("vector", "tensor", xn_pe["ck"])
                sch.on("vector", lambda v, stg=xst[j % 2]: v.scalar_tensor_tensor(
                    out=xn[:], in0=stg[:, :D], scalar=inv[:, 0:1],
                    in1=nmi[:].broadcast_to([128, D]), op0=ALU.mult, op1=ALU.add))
                ck_xn = sch.ck("vector")
                stg_readers[j] = [("vector", ck_xn)]
                if g == 0:
                    pool_wait([0, 1])
                sch.wait("tensor", "vector", ck_xn)
                for c in range(DCH):
                    sch.on("tensor", lambda tt, c=c, g=g: tt.transpose(
                        slot(c, g * 128), xn[:, c * 128:(c + 1) * 128], idn[:]))
                tck[g] = sch.ck("tensor")
                xn_pe["ck"] = tck[g]

            sch.wait("vector", "tensor", tck[1])
            if t >= 1:
                sch.wait("vector", "tensor", hs_attn_ck[t - 1])
            if t == 0:
                sch.waitd("vector", "A", vA)
            sch.rdrain("vector")
            for c in range(DCH):
                init = 0.0 if kk == 0 else carry[:, c:c + 1]
                a0 = a_sb[:, c:c + 1].broadcast_to([128, 256])
                psrc = pool[c // 4][:].rearrange("p (n x) -> p n x",
                                                 x=256)[:, c % 4, :]
                sch.on("vector", lambda v, c=c, init=init, a0=a0, psrc=psrc:
                       v.tensor_tensor_scan(out=hs[:, c * T:c * T + 256],
                                            data0=a0, data1=psrc, initial=init,
                                            op0=ALU.mult, op1=ALU.add))
            sch.rdrain("vector")
            sch.on("vector", lambda v: v.tensor_copy(
                out=carry[:].unsqueeze(2),
                in_=hs[:].rearrange("p (c t) -> p c t", t=T)[:, :, T - 1:T]))
            ck_scan = sch.ck("vector")
            pool_consumed(0, "vector", ck_scan)
            pool_consumed(1, "vector", ck_scan)

            radd_ck = {}
            for g in range(NG):
                j = NG * t + g
                sch.wait("tensor", "vector", ck_scan)
                if t == 0 and g == 0:
                    sch.wait("tensor", "scalar", ck_bc)
                sch.wait("tensor", "vector", attn_free["ck"])
                for os in range(2):
                    for c in range(DCH):
                        sch.on("tensor", lambda tt, os=os, c=c, g=g: tt.matmul(
                            pattn[os][:], hs[:, c * T + g * 128:c * T + (g + 1) * 128],
                            bcq[:, c * D + os * 512:c * D + os * 512 + 512],
                            start=(c == 0), stop=(c == DCH - 1)))
                ck_attn = sch.ck("tensor")
                if g == NG - 1:
                    hs_attn_ck[t] = ck_attn

                sch.wait("vector", "tensor", ck_attn)
                sch.waitd("vector", f"o{g}", out_dma[g])
                for os in range(2):
                    sch.on("vector", lambda v, os=os, g=g, stg=xst[j % 2]:
                           v.tensor_tensor(out=x1f[g][:, os * 512:os * 512 + 512],
                                           in0=stg[:, os * 512:os * 512 + 512],
                                           in1=pattn[os][:], op=ALU.add))
                radd_ck[g] = sch.ck("vector")
                attn_free["ck"] = radd_ck[g]
                stg_readers[j].append(("vector", radd_ck[g]))

            emit_loads(t + 1)

            qck = {}
            xn2ck = {}

            def emit_ln2(g):
                ln_stats(x1f[g], 0, D, ("vector", radd_ck[g]))
                sch.on("vector", lambda v, g=g: v.scalar_tensor_tensor(
                    out=xn2[:], in0=x1f[g][:], scalar=inv[:, 0:1],
                    in1=nmi[:].broadcast_to([128, D]), op0=ALU.mult, op1=ALU.add))
                xn2ck[g] = sch.ck("vector")

            emit_ln2(0)
            y_ck, qck[0] = quant(xn2, 0, D, xq_tm, 0,
                                 src_ck=("vector", xn2ck[0]),
                                 dst_free_ck=("tensor", hqb_pe["ck"]),
                                 hook_after_y=lambda: emit_ln2(1))
            emit_xqT(t, 0, qck[0])
            for g in range(1, NG):
                y_ck, qck[g] = quant(xn2, 0, D, xq_tm, 0,
                                     src_ck=("vector", xn2ck[g]),
                                     dst_free_ck=("tensor", hqb_pe["ck"]))

            def gu_slices(g, mid_hook=None):
                for si, (soff, sw) in enumerate(SL512):
                    pg = (pgu, pattn, pdown)[si % 3]
                    sch.wait("tensor", "scalar", xqf_ck[g])
                    prevmult = gu_mult_ck.get((g, si - 3)) if si >= 3 else (
                        gu_mult_ck.get((1 - g, NSL - 3 + si)))
                    if prevmult:
                        sch.wait("tensor", "vector", prevmult)
                    if si in (0, 1):
                        sch.wait("tensor", "vector", attn_free["ck"])
                    if si == 2:
                        sch.wait("tensor", "vector", pdown_free["ck"])
                    for c in range(DCH):
                        sch.on("tensor", lambda tt, c=c, g=g, soff=soff, sw=sw,
                               pg=pg: tt.matmul(
                            pg[0][:, :sw], xqf[:, c * T + g * 128:c * T + (g + 1) * 128],
                            wgT[:, c * I + soff:c * I + soff + sw],
                            start=(c == 0), stop=(c == DCH - 1)))
                    for c in range(DCH):
                        sch.on("tensor", lambda tt, c=c, g=g, soff=soff, sw=sw,
                               pg=pg: tt.matmul(
                            pg[1][:, :sw], xqf[:, c * T + g * 128:c * T + (g + 1) * 128],
                            wuT[:, c * I + soff:c * I + soff + sw],
                            start=(c == 0), stop=(c == DCH - 1)))
                    ck_gu = sch.ck("tensor")
                    sch.wait("scalar", "tensor", ck_gu)
                    prevmult1 = gu_mult_ck.get((g, si - 1)) if si >= 1 else (
                        gu_mult_ck.get((1 - g, NSL - 1)))
                    if prevmult1:
                        sch.wait("scalar", "vector", prevmult1)
                    sch.on("scalar", lambda s, sw=sw, pg=pg: s.activation(
                        out=sil[:, :sw], in_=pg[0][:, :sw], func=ACT.Silu))
                    ck_sil = sch.ck("scalar")
                    sch.wait("vector", "scalar", ck_sil)
                    sch.on("vector", lambda v, g=g, soff=soff, sw=sw, pg=pg:
                           v.tensor_tensor(
                        out=hbuf2[g][:, soff:soff + sw], in0=sil[:, :sw],
                        in1=pg[1][:, :sw], op=ALU.mult))
                    gu_mult_ck[(g, si)] = sch.ck("vector")
                    if si == 3 and mid_hook is not None:
                        mid_hook()
                attn_free["ck"] = max(attn_free["ck"], gu_mult_ck[(g, NSL - 1)])
                pdown_free["ck"] = max(pdown_free["ck"], gu_mult_ck[(g, NSL - 1)])

            gu_slices(0)
            emit_xqT(t, 1, qck[1])
            hq_pre = {}

            def _pre_hq0():
                y_ck, q_ck = quant(hbuf2[0], 0, IH, hqb, 0,
                                   src_ck=("vector", gu_mult_ck[(0, NSL - 1)]),
                                   dst_free_ck=("tensor", hqb_pe["ck"]))
                hq_pre["q"] = q_ck

            gu_slices(1, mid_hook=_pre_hq0)

            def emit_out(g, dck):
                sch.wait("vector", "tensor", dck)
                for dsl in range(2):
                    sch.on("vector", lambda v, g=g, dsl=dsl: v.tensor_tensor(
                        out=x1f[g][:, dsl * 512:dsl * 512 + 512],
                        in0=x1f[g][:, dsl * 512:dsl * 512 + 512],
                        in1=pdown[dsl][:], op=ALU.add))
                eck = sch.ck("vector")
                pdown_free["ck"] = max(pdown_free["ck"], eck)
                sch.wait("gpsimd", "vector", eck)
                r0g = row0 + g * 128
                out_dma[g] = sch.dma_on("gpsimd", f"o{g}", lambda e, sem,
                                        r0g=r0g, g=g: e.dma_start(
                    out=y_h[r0g:r0g + 128, :], in_=x1f[g][:, :D]).then_inc(sem, 16))

            down_cks = {}
            for g in range(NG):
                j = NG * t + g
                down_ck = 0
                for half in range(2):
                    if g == 0 and half == 0:
                        hq_ck = hq_pre["q"]
                    else:
                        hook = None
                        if g == 1 and half == 0:
                            hook = lambda: emit_out(0, down_cks[0])
                        y_ck, hq_ck = quant(hbuf2[g], half * IH, IH, hqb, 0,
                                            src_ck=("vector",
                                                    gu_mult_ck[(g, NSL - 1)]),
                                            dst_free_ck=("tensor", hqb_pe["ck"]),
                                            hook_after_y=hook)
                    for bi, nb in enumerate((4, 4, 3)):
                        bank = bi % 2
                        pool_wait([bank])
                        if bi == 0:
                            sch.wait("tensor", "vector", hq_ck)
                        for k in range(nb):
                            sch.on("tensor", lambda tt, bi=bi, k=k, bank=bank:
                                   tt.transpose(pool[bank][:, k * 256:k * 256 + 128],
                                                hqb[:, (bi * 4 + k) * 128:
                                                    (bi * 4 + k) * 128 + 128], idn[:]))
                        ck_hqT = sch.ck("tensor")
                        hqb_pe["ck"] = ck_hqT
                        sch.wait("scalar", "tensor", ck_hqT)
                        if down_ck:
                            sch.wait("scalar", "tensor", down_ck)
                        inap = pool[bank][:].rearrange("p (n x) -> p n x",
                                                       x=256)[:, 0:nb, 0:128]
                        outap = hqf[:].rearrange("p (n x) -> p n x",
                                                 x=128)[:, 0:nb, :]
                        sch.on("scalar", lambda s, i=inap, o=outap: s.copy(out=o, in_=i))
                        ck_ev = sch.ck("scalar")
                        pool_consumed(bank, "scalar", ck_ev)
                        sch.wait("tensor", "scalar", ck_ev)
                        first = (half == 0 and bi == 0)
                        last = (half == 1 and bi == 2)
                        if first:
                            sch.wait("tensor", "vector", pdown_free["ck"])
                        for k in range(nb):
                            ic = half * 11 + bi * 4 + k
                            for dsl in range(2):
                                sch.on("tensor", lambda tt, k=k, ic=ic, dsl=dsl,
                                       st=(first and k == 0),
                                       sp=(last and k == nb - 1): tt.matmul(
                                    pdown[dsl][:], hqf[:, k * 128:k * 128 + 128],
                                    wdT[:, ic * D + dsl * 512:ic * D + dsl * 512 + 512],
                                    start=st, stop=sp))
                        down_ck = sch.ck("tensor")
                down_cks[g] = down_ck
            emit_out(1, down_cks[1])

        sch.on("sync", lambda e, t0=sch.dma["o0"]: e.wait_ge(sems["dO0"], t0))
        sch.on("sync", lambda e, t1=sch.dma["o1"]: e.wait_ge(sems["dO1"], t1))

        with nc.Block() as block:
            sim_drains = SIM_DRAINS

            def run(e, eng):
                last = [None]
                my = ESEM[eng]
                for item in sch.ops[eng]:
                    kind = item[0]
                    if kind == "op":
                        r = item[1](e)
                        if r is not None:
                            last[0] = r
                    elif kind == "wait":
                        e.wait_ge(sems[ESEM[item[1]]], item[2])
                    elif kind == "waitd":
                        e.wait_ge(sems[DSEM[item[1]]], item[2])
                    elif kind == "sdrain":
                        if sim_drains and eng != "sync":
                            e.drain()
                    elif kind == "rdrain":
                        e.drain()
                    elif kind == "ck":
                        if eng == "sync":
                            e.nop().then_inc(sems[my], 1)
                        elif last[0] is not None:
                            last[0].then_inc(sems[my], 1)
                            last[0] = None
                        else:
                            e.drain().then_inc(sems[my], 1)
                    elif kind == "dma":
                        item[2](e, sems[DSEM[item[1]]])

            @block.sync
            def _(e):
                run(e, "sync")

            @block.vector
            def _(e):
                run(e, "vector")

            @block.scalar
            def _(e):
                run(e, "scalar")

            @block.tensor
            def _(e):
                run(e, "tensor")

            @block.gpsimd
            def _(e):
                run(e, "gpsimd")

    return nc


_NC_CACHE = {}


def kernel(L0_attn_inputs, L0_attn_A, L0_attn_BC, L0_ffn_w_gate, L0_ffn_w_up,
           L0_ffn_w_down):
    x = np.ascontiguousarray(np.asarray(L0_attn_inputs, dtype=np.float32))
    A = np.ascontiguousarray(np.asarray(L0_attn_A, dtype=np.float32))
    BC = np.ascontiguousarray(np.asarray(L0_attn_BC, dtype=np.float32))
    WG = np.ascontiguousarray(np.asarray(L0_ffn_w_gate, dtype=np.float32))
    WU = np.ascontiguousarray(np.asarray(L0_ffn_w_up, dtype=np.float32))
    WD = np.ascontiguousarray(np.asarray(L0_ffn_w_down, dtype=np.float32))

    if "nc" not in _NC_CACHE:
        _NC_CACHE["nc"] = build_program()
    nc = _NC_CACHE["nc"]

    in_maps = []
    for c in range(NCORES):
        xs = np.ascontiguousarray(x[c * BPC:(c + 1) * BPC].reshape(BPC * S, D))
        in_maps.append({"x": xs, "A": A, "BC": BC, "WG": WG, "WU": WU, "WD": WD})

    res = run_bass_kernel_spmd(nc, in_maps, core_ids=list(range(NCORES)))
    out = np.concatenate([r["y"].reshape(BPC, S, D) for r in res.results], axis=0)
    return out
